# version 25
# baseline (speedup 1.0000x reference)
import contextlib
import math

import numpy as np

import concourse.bass as bass
import concourse.mybir as mybir
from concourse.bass_utils import run_bass_kernel_spmd

F32 = mybir.dt.float32
BF16 = mybir.dt.bfloat16
I32 = mybir.dt.int32
AF = mybir.ActivationFunctionType
ALU = mybir.AluOpType

B, L, D = 16, 2048, 2048
NCORES = 8
BPC = B // NCORES
T = 512
TILES_PER_SEQ = L // T
NT = BPC * TILES_PER_SEQ
NCH = D // 128
H = 16
DK = 128
CPT = T // 128
DKSCALE = DK ** -0.5
FFN_C = 0.5 / math.sqrt(8.0)
EPS = 1e-5

ENGS = ("t", "v", "s", "d", "g")
PIPELINE_LN1 = False


NLANES = 8


class Em:
    def __init__(self):
        self.ops = {e: [] for e in ENGS}
        self.cnt = {e: 0 for e in ENGS}
        self.lane_cnt = {}
        for q in ("d", "g"):
            for i in range(NLANES):
                self.lane_cnt[q + str(i)] = 0
        self.lane_idx = {"d": 0, "g": 0}

    def op(self, eng, fn, waits=(), inc=0, inc_sem=None):
        ev = None
        if inc:
            if inc_sem is None:
                self.cnt[eng] += inc
                ev = (eng, self.cnt[eng])
            else:
                self.lane_cnt[inc_sem] += inc
                ev = (inc_sem, self.lane_cnt[inc_sem])
        self.ops[eng].append((fn, tuple(w for w in waits if w is not None),
                              inc, inc_sem or eng))
        return ev

    def t(self, fn, waits=(), inc=1):
        return self.op("t", fn, waits, inc)

    def v(self, fn, waits=(), inc=1):
        return self.op("v", fn, waits, inc)

    def s(self, fn, waits=(), inc=1):
        return self.op("s", fn, waits, inc)

    def _dma(self, q, fn, waits):
        l = self.lane_idx[q]
        self.lane_idx[q] = (l + 1) % NLANES
        key = q + str(l)
        w = list(waits)
        prev = self.lane_cnt[key]
        if prev:
            w.append((key, prev))
        return self.op(q, fn, w, inc=16, inc_sem=key)

    def d(self, fn, waits=(), inc=16):
        return self._dma("d", fn, waits)

    def g(self, fn, waits=(), inc=16):
        return self._dma("g", fn, waits)


class Kern:

    def __init__(self, nc, es, taps):
        self.nc = nc
        self.em = Em()
        E = es.enter_context
        self.x_in = nc.dram_tensor("x", [BPC, D, L], F32, kind="ExternalInput").ap()
        self.wT = [nc.dram_tensor(n, [NCH, 128, NCH * 128], F32, kind="ExternalInput").ap()
                   for n in ("wqT", "wkT", "wvT", "woT")]
        self.facs = nc.dram_tensor("factors", [8, NCH, 128, NCH * 128], F32,
                                   kind="ExternalInput").ap()
        self.fqkT = [nc.dram_tensor(n, [DK, DK], F32, kind="ExternalInput").ap()
                     for n in ("fq1T", "fq2T", "fk1T", "fk2T")]
        self.bin = [nc.dram_tensor(n, [DK, 1], F32, kind="ExternalInput").ap()
                    for n in ("bq1", "bq2", "bk1", "bk2")]
        self.nw_in = nc.dram_tensor("nw", [DK, 1], F32, kind="ExternalInput").ap()
        self.fb_in = nc.dram_tensor("fbias", [128, NCH], F32, kind="ExternalInput").ap()
        self.y_out = nc.dram_tensor("y", [BPC, D, L], F32, kind="ExternalOutput").ap()
        _wscr_kind = "ExternalOutput" if (taps and "wscr" in taps) else "Internal"
        self.wscr = nc.dram_tensor("wscr", [80, 128, NCH * 128], BF16,
                                   kind=_wscr_kind).ap()
        self.tap_out = {}
        self.taps = taps or {}
        for lbl, (shape, dt) in self.taps.items():
            if lbl == "wscr":
                continue
            self.tap_out[lbl] = nc.dram_tensor(
                "tap_" + lbl, shape, dt, kind="ExternalOutput").ap()

        sb = nc.sbuf_tensor
        self.ident = E(sb("ident_bf", [128, 128], BF16))
        self.iota = E(sb("iota_i", [128, 128], I32))
        self.mask = E(sb("mask_bf", [128, 128], BF16))
        self.ones_col = E(sb("ones_col", [128, 1], BF16))
        self.ones_row = E(sb("ones_row", [1, 128], BF16))
        self.zero_col = E(sb("zero_col", [128, 1], F32))
        self.eps_col = E(sb("eps_col", [128, 1], F32))
        self.fqk = [E(sb("fqk%d" % i, [128, 128], BF16)) for i in range(4)]
        self.bcols = [E(sb("bcol%d" % i, [128, 1], F32)) for i in range(4)]
        self.nw = E(sb("nw_col", [128, 1], F32))
        self.hbias = E(sb("hbias", [128, NCH], F32))
        self.xs = [E(sb("xs%d" % i, [128, NCH * T], F32)) for i in range(2)]
        self.xb = [E(sb("xb%d" % i, [128, NCH * T], BF16)) for i in range(2)]
        self.o_all = E(sb("o_all", [128, NCH * T], BF16))
        self.vh = [E(sb("vh%d" % i, [128, T], BF16)) for i in range(2)]
        self.q0h = [E(sb("q0h%d" % i, [128, T], BF16)) for i in range(2)]
        self.k0h = [E(sb("k0h%d" % i, [128, T], BF16)) for i in range(2)]
        self.vfh = [E(sb("vfh%d" % i, [128, T], BF16)) for i in range(2)]
        self.qh = [E(sb("qh%d" % i, [128, T], BF16)) for i in range(2)]
        self.kfmh = [E(sb("kfmh%d" % i, [128, T], BF16)) for i in range(2)]
        self.ktmh = [E(sb("ktmh%d" % i, [128, T], BF16)) for i in range(2)]
        self.g1f = [E(sb("g1f%d" % i, [128, T], F32)) for i in range(2)]
        self.sTm = [E(sb("sTm%d" % i, [128, 128], BF16)) for i in range(2)]
        self.state_f = E(sb("state_f", [128, H * DK], F32))
        self.state_b = [E(sb("state_b%d" % i, [128, H * DK], BF16)) for i in range(2)]
        self.NSLOT = 8
        self.wslot = [E(sb("wslot%d" % i, [128, NCH * 128], BF16))
                      for i in range(self.NSLOT)]
        self.tmp_e = E(sb("tmp_e", [128, T], F32))
        self.mbc = [E(sb("mbc%d" % i, [128, T], F32)) for i in range(2)]
        self.rbc = [E(sb("rbc%d" % i, [128, T], F32)) for i in range(2)]
        self.rrbc = E(sb("rrbc", [128, T], F32))
        self.xsq = [E(sb("xsq%d" % i, [128, T], BF16)) for i in range(2)]
        self.osq = E(sb("osq", [128, T], BF16))
        self.ones_sq = E(sb("ones_sq", [128, 128], BF16))
        self.vbc = E(sb("vbc", [128, T], F32))
        self.lnc_col = E(sb("lnc_col", [128, 1], F32))

        self.fstage = self.xs[0]
        self.facc_ap = self.xs[1][:, 0:2048]

        self.pA = E(nc.psum_tensor("pA", [128, 512], F32))
        self.pB = E(nc.psum_tensor("pB", [128, 512], F32))
        self.pTT = E(nc.psum_tensor("pTT", [128, 1024], BF16))
        self.pS = [E(nc.psum_tensor("pS0", [128, 512], F32)),
                   E(nc.psum_tensor("pS1", [128, 512], F32))]
        self.pO = E(nc.psum_tensor("pO", [128, 512], F32))
        self.pK = [E(nc.psum_tensor("pK0", [128, 512], F32)),
                   E(nc.psum_tensor("pK1", [128, 512], F32))]
        self.pAB = [self.pA, self.pB]

        self.sems = {e: E(nc.semaphore("sem_" + e)) for e in ENGS}
        for q in ("d", "g"):
            for i in range(NLANES):
                key = q + str(i)
                self.sems[key] = E(nc.semaphore("sem_" + key))

        self.guard = {}
        self.slot_free = [None] * self.NSLOT
        self.slot_store = [None] * self.NSLOT
        self.scr_built = {}
        self.fsl_free = [None, None, None]
        self.ev_fcast = None
        self.slot_rr = 0
        self.ev_store = [None, None]
        self.ev_ffn2_mml = None
        self.ev_o_consumed = None
        self.ev_attn_done = None
        self.ev_last_cast = None
        self.cast_ev = [[None] * H, [None] * H]
        self.inter_ev = [[None] * H, [None] * H]
        self.xsq_guard = [None, None]
        self.sT_guard = [None, None]
        self.sTm_free = [None, None]
        self.vh_free = [None, None]
        self.kv_guard = [None, None]
        self.hfree = {n: [None, None] for n in
                      ("q0h", "k0h", "vfh", "qh", "kfm", "ktm", "g1f")}
        self.sw = [None, None]
        self.last_v0 = None
        self.osq_free = None
        self.rrbc_free = None
        self.rbc_free = [None, None]
        self.mbc_free = [None, None]
        self.cur_xb = None
        self.cur_rbc = None
        self.cur_par = 0
        self.next_ln1 = None
        self.pre_xload = None
        self.ev_xt = [None] * NCH
        self.omult = [None] * H

    def ch(self, buf, i, w=T):
        return buf[:, i * w:(i + 1) * w]

    def c128(self, buf, c):
        return buf[:, c * 128:(c + 1) * 128]

    def _fsl_ap(self, s):
        return self.xs[1][:, (s + 1) * 2048:(s + 2) * 2048]

    def _build_fcol(self, j, s_):
        em = self.em
        k = self

        def ld(r):
            sl2 = r % 3
            src_ = k.facs[r, j]
            q = em.g if r % 2 == 0 else em.d
            return q(lambda g, sr=src_, s2=sl2: g.dma_start(
                k._fsl_ap(s2), sr), waits=[k.fsl_free[sl2]])

        ev_ld = [ld(0), ld(1), ld(2)]
        ev_add = em.v(lambda v: v.tensor_tensor(
            k.facc_ap, k._fsl_ap(0), k._fsl_ap(1), ALU.add),
            waits=[ev_ld[0], ev_ld[1], k.ev_fcast])
        k.fsl_free[0] = ev_add
        k.fsl_free[1] = ev_add
        for r in range(2, 8):
            if r + 1 < 8:
                ev_ld.append(ld(r + 1))
            sl2 = r % 3
            ev_add = em.v(lambda v, s2=sl2: v.tensor_tensor(
                k.facc_ap, k.facc_ap, k._fsl_ap(s2), ALU.add),
                waits=[ev_ld[r], ev_add])
            k.fsl_free[sl2] = ev_add
        ev_c = em.g(lambda g, sl=s_: g.dma_start(
            k.wslot[sl][:, :], k.facc_ap),
            waits=[ev_add, k.slot_free[s_], k.slot_store[s_]])
        k.ev_fcast = ev_c
        return ev_c

    def load_col(self, mat, j):
        s_ = self.slot_rr
        self.slot_rr = (s_ + 1) % self.NSLOT
        row = mat * 16 + j
        if (mat, j) not in self.scr_built:
            if mat == 4:
                ev = self._build_fcol(j, s_)
            else:
                src_ = self.wT[mat][j]
                ev = self.em.g(lambda g, sr=src_, sl=s_: g.dma_start(
                    self.wslot[sl][:, :], sr),
                    waits=[self.slot_free[s_], self.slot_store[s_]])
            ev_st = self.em.d(lambda d, sl=s_, r_=row: d.dma_start(
                self.wscr[r_], self.wslot[sl][:, :]), waits=[ev])
            self.slot_store[s_] = ev_st
            self.scr_built[(mat, j)] = ev_st
            return s_, ev
        q = self.em.d
        ev = q(lambda d, r_=row, sl=s_: d.dma_start(
            self.wslot[sl][:, :], self.wscr[r_]),
            waits=[self.slot_free[s_], self.slot_store[s_],
                   self.scr_built[(mat, j)]])
        self.slot_store[s_] = None
        return s_, ev

    def tap(self, lbl, buf_ap, waits):
        if lbl in self.tap_out:
            return self.em.d(lambda d, b_=buf_ap, o_=self.tap_out[lbl]:
                             d.dma_start(o_, b_), waits=waits)
        return None


def _init(k):
    em, nc = k.em, k.nc
    ev = em.g(lambda g: g.iota(k.iota[:, :], [[1, 128]], base=0,
                               channel_multiplier=-1), inc=1)
    em.g(lambda g: g.memset(k.ones_col[:, :], 1.0), inc=1)
    em.g(lambda g: g.memset(k.ones_row[:, :], 1.0), inc=1)
    em.g(lambda g: g.memset(k.zero_col[:, :], 0.0), inc=1)
    em.g(lambda g: g.memset(k.eps_col[:, :], EPS), inc=1)
    em.g(lambda g: g.memset(k.ones_sq[:, :], 1.0), inc=1)
    em.g(lambda g: g.memset(k.lnc_col[:, :], math.log(FFN_C)), inc=1)
    em.g(lambda g: g.memset(k.state_f[:, :], 0.0), inc=1)
    em.g(lambda g: g.memset(k.state_b[0][:, :], 0.0), inc=1)
    ev_gi = em.g(lambda g: g.memset(k.state_b[1][:, :], 0.0), inc=1)
    k.ev_mask = em.v(lambda v: v.tensor_scalar(
        k.mask[:, :], k.iota[:, :], 0, None, ALU.is_ge), waits=[ev_gi])
    k.ev_ident = em.v(lambda v: v.tensor_scalar(
        k.ident[:, :], k.iota[:, :], 0, None, ALU.is_equal), waits=[ev_gi])
    k.ev_ginit = ev_gi
    k.cast_ev = [[ev_gi] * H, [ev_gi] * H]
    k.ev_last_cast = ev_gi

    evl = []
    for i in range(4):
        evl.append(em.d(lambda d, i_=i: d.dma_start(
            k.vbc[:, i_ * 128:(i_ + 1) * 128], k.fqkT[i_])))
    evb = []
    for i in range(4):
        evb.append(em.d(lambda d, i_=i: d.dma_start(k.bcols[i_][:, :], k.bin[i_])))
    ev_nw = em.d(lambda d: d.dma_start(k.nw[:, :], k.nw_in))
    ev_fb = em.d(lambda d: d.dma_start(k.hbias[:, :], k.fb_in))
    em.s(lambda s: s.mul(k.fqk[0][:, :], k.vbc[:, 0:128], DKSCALE), waits=[evl[0]])
    for i in (1, 2, 3):
        em.s(lambda s, i_=i: s.copy(
            k.fqk[i_][:, :], k.vbc[:, i_ * 128:(i_ + 1) * 128]), waits=[evl[i]])
    ev_fqk = em.s(lambda s: s.mul(k.bcols[0][:, :], k.bcols[0][:, :], DKSCALE),
                  waits=[evb[0]])
    k.ev_hb = em.s(lambda s: s.mul(k.hbias[:, :], k.hbias[:, :], 0.5),
                   waits=[ev_fb, evb[1], evb[2], evb[3], ev_nw])
    k.ev_smallw = ev_fqk
    src0 = k.x_in[0, :, 0:T].rearrange("(i p) t -> p i t", p=128)
    k.ev_xload0 = em.d(lambda d, s_=src0: d.dma_start(
        k.xs[0][:, :].rearrange("p (i t) -> p i t", t=T), s_))


def _weight_prep(k):
    pass


def _ln_pass(k, src_buf, ffn_scaled, ev_src_ready, xb_guard, par):
    em = k.em
    g = k.guard
    xb = k.xb[par]
    mbc = k.mbc[par]
    rbc = k.rbc[par]
    ev_cast = [None] * NCH
    ev_statmm = [None] * NCH
    for i in range(NCH):
        if isinstance(ev_src_ready, list):
            w_src = ev_src_ready[i]
        else:
            w_src = ev_src_ready
        ev_cast[i] = em.s(lambda s, i_=i, sb=src_buf: s.copy(
            k.ch(xb, i_), k.ch(sb, i_)), waits=[w_src, xb_guard])
        ev_sq = em.s(lambda s, i_=i: s.activation(
            k.xsq[i_ % 2][:, :], k.ch(xb, i_), AF.Square,
            bias=k.zero_col[:, :], scale=1.0), waits=[k.xsq_guard[i % 2]])
        em.t(lambda tt, i_=i: tt.matmul(
            k.pK[0][:, 0:T], k.ones_sq[:, :], k.ch(xb, i_),
            start=(i_ == 0), stop=(i_ == NCH - 1)),
            waits=[ev_cast[i], g.get("pK0"), k.ev_ginit], inc=0)
        ev_statmm[i] = em.t(lambda tt, i_=i: tt.matmul(
            k.pK[1][:, 0:T], k.ones_sq[:, :], k.xsq[i_ % 2][:, :],
            start=(i_ == 0), stop=(i_ == NCH - 1)),
            waits=[ev_sq, g.get("pK1"), k.ev_ginit])
        k.xsq_guard[i % 2] = ev_statmm[i]
    ev_stats = ev_statmm[NCH - 1]
    ev_mbc = em.s(lambda s: s.mul(mbc[:, :], k.pK[0][:, 0:T], 1.0 / D),
                  waits=[ev_stats, k.mbc_free[par]])
    g["pK0"] = ev_mbc
    ev_msq = em.s(lambda s: s.activation(
        k.vbc[:, :], mbc[:, :], AF.Square, bias=k.zero_col[:, :], scale=1.0))
    ev_vbc = em.v(lambda v: v.scalar_tensor_tensor(
        k.vbc[:, :], k.pK[1][:, 0:T], 1.0 / D, k.vbc[:, :],
        ALU.mult, ALU.subtract), waits=[ev_msq, ev_stats])
    g["pK1"] = ev_vbc
    ev_lnv = em.s(lambda s: s.activation(
        k.vbc[:, :], k.vbc[:, :], AF.Ln, bias=k.eps_col[:, :], scale=1.0),
        waits=[ev_vbc])
    bias_ap = k.lnc_col if ffn_scaled else k.zero_col
    ev_rbc = em.s(lambda s, ba=bias_ap: s.activation(
        rbc[:, :], k.vbc[:, :], AF.Exp, bias=ba[:, :], scale=-0.5),
        waits=[ev_lnv, k.rbc_free[par]])
    ev_xt = [None] * NCH
    for i in range(NCH):
        ev_xt[i] = em.v(lambda v, i_=i, sb=src_buf: v.tensor_tensor(
            k.ch(xb, i_), k.ch(sb, i_), mbc[:, :], ALU.subtract),
            waits=[ev_mbc, ev_statmm[i]])
    k.mbc_free[par] = ev_xt[NCH - 1]
    return ev_xt, ev_rbc


def _ffn_sweep(k, mat, xcur, xb, ev_xt, rbc, par):
    em = k.em
    ev_evac = [None] * NCH
    ev_mml = None
    for j in range(NCH):
        sl, ev_w = k.load_col(mat, j)
        pj = k.pAB[j % 2]
        bank = "pA" if j % 2 == 0 else "pB"
        for i in range(NCH):
            w = [ev_xt[i]]
            if i == 0:
                w += [ev_w, k.guard.get(bank)]
            ev_mml = em.t(lambda tt, i_=i, sl_=sl, p_=pj: tt.matmul(
                p_[:, :], k.wslot[sl_][:, i_ * 128:(i_ + 1) * 128],
                k.ch(xb, i_), start=(i_ == 0), stop=(i_ == NCH - 1)),
                waits=w, inc=1 if i == NCH - 1 else 0)
        k.slot_free[sl] = ev_mml
        em.v(lambda v, p_=pj: v.tensor_tensor(
            k.tmp_e[:, :], p_[:, :], rbc[:, :], ALU.mult),
            waits=[ev_mml, k.ev_hb], inc=0)
        ev_evac[j] = em.v(lambda v, j_=j, xc=xcur: v.scalar_tensor_tensor(
            k.ch(xc, j_), k.tmp_e[:, :], k.hbias[:, j_:j_ + 1],
            k.ch(xc, j_), ALU.add, ALU.add))
        k.guard[bank] = ev_evac[j]
    k.rbc_free[par] = ev_evac[NCH - 1]
    return ev_evac, ev_mml


def _head_sweeps(k, h):
    em = k.em
    g = k.guard
    par = h % 2
    xbc = k.cur_xb
    rbcc = k.cur_rbc

    def sweep(mat, pbank, bankkey, extra0):
        sl, ev_w = k.load_col(mat, h)
        ev = None
        for half in range(2):
            for i in range(half * 8, half * 8 + 8):
                w = [k.ev_xt[i]]
                if i == 0:
                    w += [ev_w, g.get(bankkey)] + extra0
                ev = em.t(lambda tt, i_=i, sl_=sl, p_=pbank: tt.matmul(
                    p_[:, :], k.wslot[sl_][:, i_ * 128:(i_ + 1) * 128],
                    k.ch(xbc, i_), start=(i_ == 0), stop=(i_ == NCH - 1)),
                    waits=w, inc=1 if i == NCH - 1 else 0)
            if half == 1:
                k.slot_free[sl] = ev
            yield ev

    it = sweep(0, k.pA, "pA", [])
    next(it)
    yield
    ev_q0 = next(it)
    ev_q0e = em.v(lambda v: v.tensor_tensor(
        k.q0h[par][:, :], k.pA[:, :], rbcc[:, :], ALU.mult),
        waits=[ev_q0, k.hfree["q0h"][par]])
    g["pA"] = ev_q0e
    yield
    it = sweep(1, k.pB, "pB", [])
    next(it)
    yield
    ev_k0 = next(it)
    ev_k0e = em.v(lambda v: v.tensor_tensor(
        k.k0h[par][:, :], k.pB[:, :], rbcc[:, :], ALU.mult),
        waits=[ev_k0, k.hfree["k0h"][par]])
    g["pB"] = ev_k0e
    yield
    it = sweep(2, k.pA, "pA", [])
    next(it)
    yield
    ev_v0 = next(it)
    ev_vfme = em.v(lambda v: v.tensor_tensor(
        k.vfh[par][:, :], k.pA[:, :], rbcc[:, :], ALU.mult),
        waits=[ev_v0, k.hfree["vfh"][par]])
    g["pA"] = ev_vfme
    k.rbc_free[k.cur_par] = ev_vfme
    k.sw[par] = (ev_q0e, ev_k0e, ev_vfme)
    k.last_v0 = ev_v0


def _head_attn(k, h):
    em = k.em
    g = k.guard
    par = h % 2
    ev_q0e, ev_k0e, ev_vfme = k.sw[par]
    q0h, k0h, vfh = k.q0h[par], k.k0h[par], k.vfh[par]
    qh, kfmh, ktmh, g1f, vhb = (k.qh[par], k.kfmh[par], k.ktmh[par],
                                k.g1f[par], k.vh[par])
    ev_fq1 = em.t(lambda tt: tt.matmul(
        k.pS[0][:, :], k.fqk[0][:, :], q0h[:, :], start=True, stop=True),
        waits=[ev_q0e, k.sT_guard[0], k.ev_smallw])
    ev_fq2 = em.t(lambda tt: tt.matmul(
        k.pS[1][:, :], k.fqk[1][:, :], q0h[:, :], start=True, stop=True),
        waits=[k.sT_guard[1]])
    ev_g1 = em.s(lambda s: s.activation(
        g1f[:, :], k.pS[0][:, :], AF.Identity, bias=k.bcols[0][:, :],
        scale=1.0), waits=[ev_fq1, k.hfree["g1f"][par]])
    ev_qh = em.v(lambda v: v.scalar_tensor_tensor(
        qh[:, :], k.pS[1][:, :], k.bcols[1][:, :], g1f[:, :],
        ALU.add, ALU.mult),
        waits=[ev_fq2, ev_g1, k.hfree["qh"][par]])
    yield
    ev_fk1 = em.t(lambda tt: tt.matmul(
        k.pS[0][:, :], k.fqk[2][:, :], k0h[:, :], start=True, stop=True),
        waits=[ev_g1, ev_k0e])
    ev_fk2 = em.t(lambda tt: tt.matmul(
        k.pS[1][:, :], k.fqk[3][:, :], k0h[:, :], start=True, stop=True),
        waits=[ev_qh])
    ev_g1k = em.s(lambda s: s.activation(
        g1f[:, :], k.pS[0][:, :], AF.Identity, bias=k.bcols[2][:, :],
        scale=1.0), waits=[ev_fk1, ev_qh])
    ev_kfm = em.v(lambda v: v.scalar_tensor_tensor(
        kfmh[:, :], k.pS[1][:, :], k.bcols[3][:, :], g1f[:, :],
        ALU.add, ALU.mult),
        waits=[ev_fk2, ev_g1k, k.hfree["kfm"][par]])
    k.hfree["g1f"][par] = ev_kfm
    k.hfree["q0h"][par] = ev_fq2
    k.hfree["k0h"][par] = ev_fk2
    yield
    for c in range(CPT):
        em.t(lambda tt, c_=c: tt.transpose(
            k.pTT[:, c_ * 128:(c_ + 1) * 128], k.c128(kfmh, c_), k.ident[:, :]),
            waits=[ev_kfm, k.ev_ident, g.get("pTT")] if c == 0 else [], inc=0)
    ev_tr = None
    for c in range(CPT):
        ev_tr = em.t(lambda tt, c_=c: tt.transpose(
            k.pTT[:, 512 + c_ * 128:512 + (c_ + 1) * 128],
            k.c128(vfh, c_), k.ident[:, :]),
            waits=[ev_vfme] if c == 0 else [], inc=1 if c == CPT - 1 else 0)
    k.hfree["vfh"][par] = ev_tr
    ev_ktm = em.v(lambda v: v.tensor_copy(ktmh[:, :], k.pTT[:, 0:512]),
                  waits=[ev_tr, k.hfree["ktm"][par]])
    ev_vtm = em.v(lambda v: v.tensor_copy(
        vhb[:, :], k.pTT[:, 512:1024]), waits=[k.hfree["vh"][par] if False
                                               else k.vh_free[par]])
    g["pTT"] = ev_vtm
    ev_sT = [None] * CPT
    ev_maskd = [None] * CPT
    ev_inter = [None] * CPT
    ev_intra = [None] * CPT
    ev_addv = [None] * CPT
    ev_cast = [None] * CPT
    for c in range(CPT):
        yield
        gp = c % 2
        w_extra = [ev_g1k] if c == 0 else ([ev_kfm] if c == 1 else [])
        ev_sT[c] = em.t(lambda tt, c_=c, g_=gp: tt.matmul(
            k.pS[g_][:, 0:128], k.c128(kfmh, c_), k.c128(qh, c_),
            start=True, stop=True),
            waits=[ev_qh, ev_kfm, k.sT_guard[gp]] + w_extra)
        ev_maskd[c] = em.v(lambda v, c_=c, g_=gp: v.tensor_tensor(
            k.sTm[g_][:, :], k.pS[g_][:, 0:128], k.mask[:, :], ALU.mult),
            waits=[ev_sT[c], k.ev_mask, k.sTm_free[gp]])
        k.sT_guard[gp] = ev_maskd[c]
        ev_kv = em.t(lambda tt, c_=c, g_=gp: tt.matmul(
            k.pK[g_][:, 0:128], k.c128(ktmh, c_), k.c128(vhb, c_),
            start=True, stop=True),
            waits=[ev_ktm, ev_vtm, k.kv_guard[gp], g.get("pK" + str(gp))])
        g["pK" + str(gp)] = None
        po_guard = None
        if c == 0:
            po_guard = k.omult[h - 1] if h > 0 else (g.get("pO") or k.ev_o_consumed)
        ev_inter[c] = em.t(lambda tt, c_=c, h_=h, sb=k.state_b[c % 2]: tt.matmul(
            k.c128(k.pO, c_), sb[:, h_ * 128:(h_ + 1) * 128], k.c128(qh, c_),
            start=True, stop=False),
            waits=[k.cast_ev[c % 2][h], po_guard])
        k.inter_ev[c % 2][h] = ev_inter[c]
        ev_intra[c] = em.t(lambda tt, c_=c, g_=gp: tt.matmul(
            k.c128(k.pO, c_), k.c128(vhb, c_), k.sTm[g_][:, :],
            start=False, stop=True), waits=[ev_maskd[c]])
        k.sTm_free[gp] = ev_intra[c]
        ev_addv[c] = em.v(lambda v, h_=h, g_=gp: v.tensor_tensor(
            k.state_f[:, h_ * 128:(h_ + 1) * 128],
            k.state_f[:, h_ * 128:(h_ + 1) * 128],
            k.pK[g_][:, 0:128], ALU.add),
            waits=[ev_kv, ev_cast[c - 1] if c > 0 else k.ev_last_cast])
        k.kv_guard[gp] = ev_addv[c]
        ev_cast[c] = em.s(lambda s, c_=c, h_=h: s.copy(
            k.state_b[(c_ + 1) % 2][:, h_ * 128:(h_ + 1) * 128],
            k.state_f[:, h_ * 128:(h_ + 1) * 128]),
            waits=[ev_addv[c], k.inter_ev[(c + 1) % 2][h]])
        k.cast_ev[(c + 1) % 2][h] = ev_cast[c]
    k.hfree["kfm"][par] = ev_sT[CPT - 1]
    k.ev_last_cast = ev_cast[CPT - 1]
    k.hfree["qh"][par] = ev_inter[CPT - 1]
    k.hfree["ktm"][par] = ev_inter[CPT - 1]
    k.vh_free[par] = ev_intra[CPT - 1]
    k.ev_attn_done = ev_intra[CPT - 1]
    yield
    ev_osq = em.s(lambda s: s.activation(
        k.osq[:, :], k.pO[:, :], AF.Square, bias=k.zero_col[:, :], scale=1.0),
        waits=[ev_intra[CPT - 1], k.osq_free])
    ev_ssq = em.t(lambda tt: tt.matmul(
        k.pK[1][:, 0:T], k.ones_sq[:, :], k.osq[:, :], start=True, stop=True),
        waits=[ev_osq, k.kv_guard[1], g.get("pK1")])
    g["pK1"] = None
    k.osq_free = ev_ssq
    ev_lnr = em.s(lambda s: s.activation(
        k.rrbc[:, :], k.pK[1][:, 0:T], AF.Ln, bias=k.eps_col[:, :],
        scale=1.0 / DK), waits=[ev_ssq, k.rrbc_free])
    k.kv_guard[1] = ev_lnr
    ev_rrbc = em.s(lambda s: s.activation(
        k.rrbc[:, :], k.rrbc[:, :], AF.Exp, bias=k.zero_col[:, :],
        scale=-0.5), waits=[ev_lnr])
    k.omult[h] = em.v(lambda v, h_=h: v.scalar_tensor_tensor(
        k.ch(k.o_all, h_), k.pO[:, :], k.nw[:, :], k.rrbc[:, :],
        ALU.mult, ALU.mult),
        waits=[ev_rrbc, k.ev_o_consumed, k.ev_hb])
    k.rrbc_free = k.omult[h]
    g["pO"] = k.omult[h]


def _tile(k, t):
    em = k.em
    b = t // TILES_PER_SEQ
    t0 = (t % TILES_PER_SEQ) * T
    par = t % 2
    xcur = k.xs[par]
    if t == 0:
        ev_xload = k.ev_xload0
    elif PIPELINE_LN1:
        ev_xload = k.pre_xload
    else:
        srcx = k.x_in[b, :, t0:t0 + T].rearrange("(i p) t -> p i t", p=128)
        ev_xload = em.d(lambda d, s_=srcx, xc=xcur: d.dma_start(
            xc[:, :].rearrange("p (i t) -> p i t", t=T), s_),
            waits=[k.ev_store[par]])
    if t > 0 and t % TILES_PER_SEQ == 0:
        em.v(lambda v: v.memset(k.state_f[:, :], 0.0),
             waits=[k.ev_last_cast, k.ev_attn_done], inc=0)
        ev_z = em.v(lambda v: v.memset(k.state_b[0][:, :], 0.0))
        k.cast_ev = [[ev_z] * H, [ev_z] * H]
        k.ev_last_cast = ev_z
    if t == 0 or not PIPELINE_LN1:
        ev_xt1, ev_rbc1 = _ln_pass(k, xcur, True, ev_xload,
                                   k.ev_ffn2_mml, par)
    else:
        ev_xt1, ev_rbc1 = k.next_ln1
    ev_ffn1_evac, ev_ffn1_mml = _ffn_sweep(k, 4, xcur, k.xb[par], ev_xt1,
                                           k.rbc[par], par)
    if t == 0:
        k.ev_store[1] = k.ev_fcast
    if t == 0:
        k.tap("x2", xcur[:, :], [ev_ffn1_evac[NCH - 1]])
        k.tap("rbc1", k.rbc[par][:, :], [ev_rbc1])
        k.tap("xt1", k.xb[par][:, :], [ev_xt1[NCH - 1]])
    ev_xt2, ev_rbc2 = _ln_pass(k, xcur, False, ev_ffn1_evac, ev_ffn1_mml, par)
    k.cur_xb = k.xb[par]
    k.cur_rbc = k.rbc[par]
    k.cur_par = par
    k.ev_xt = ev_xt2
    def _drain(gen):
        if gen is not None:
            for _ in gen:
                pass

    def _drive(a, b):
        while True:
            if a is not None:
                try:
                    next(a)
                except StopIteration:
                    a = None
            if b is not None:
                try:
                    next(b)
                except StopIteration:
                    b = None
            if a is None and b is None:
                return

    _drain(_head_sweeps(k, 0))
    for h in range(H):
        ga = _head_sweeps(k, h + 1) if h + 1 < H else None
        gb = _head_attn(k, h)
        _drive(ga, gb)
    ev_v0_last = k.last_v0
    k.guard["pK0"] = k.kv_guard[0]
    k.guard["pK1"] = k.kv_guard[1]
    if PIPELINE_LN1 and t + 1 < NT:
        npar = (t + 1) % 2
        nb = (t + 1) // TILES_PER_SEQ
        nt0 = ((t + 1) % TILES_PER_SEQ) * T
        nsrc = k.x_in[nb, :, nt0:nt0 + T].rearrange("(i p) t -> p i t", p=128)
        ev_nx = em.d(lambda d, s_=nsrc, xc=k.xs[npar]: d.dma_start(
            xc[:, :].rearrange("p (i t) -> p i t", t=T), s_),
            waits=[k.ev_store[npar]])
        k.pre_xload = ev_nx
        k.next_ln1 = _ln_pass(k, k.xs[npar], True, ev_nx,
                              k.ev_ffn2_mml, npar)
    ev_wo_evac = [None] * NCH
    ev_wo_mml = None
    for j in range(NCH):
        sl, ev_w = k.load_col(3, j)
        pj = k.pAB[j % 2]
        bank = "pA" if j % 2 == 0 else "pB"
        for i in range(NCH):
            w = [k.omult[i]]
            if i == 0:
                w += [ev_w, k.guard.get(bank)]
            ev_wo_mml = em.t(lambda tt, i_=i, sl_=sl, p_=pj: tt.matmul(
                p_[:, :], k.wslot[sl_][:, i_ * 128:(i_ + 1) * 128],
                k.ch(k.o_all, i_), start=(i_ == 0), stop=(i_ == NCH - 1)),
                waits=w, inc=1 if i == NCH - 1 else 0)
        k.slot_free[sl] = ev_wo_mml
        ev_wo_evac[j] = em.v(lambda v, j_=j, p_=pj, xc=xcur: v.tensor_tensor(
            k.ch(xc, j_), p_[:, :], k.ch(xc, j_), ALU.add),
            waits=[ev_wo_mml])
        k.guard[bank] = ev_wo_evac[j]
    k.ev_o_consumed = ev_wo_mml
    k.tap("x3", xcur[:, :], [ev_wo_evac[NCH - 1]])
    if t == 0:
        k.tap("x3", xcur[:, :], [ev_wo_evac[NCH - 1]])
        k.tap("oall", k.o_all[:, :], [k.omult[H - 1]])
    ev_xt3, ev_rbc3 = _ln_pass(k, xcur, True, ev_wo_evac, ev_v0_last, par)
    ev_ffn2_evac, k.ev_ffn2_mml = _ffn_sweep(k, 4, xcur, k.xb[par], ev_xt3,
                                             k.rbc[par], par)
    dst = k.y_out[b, :, t0:t0 + T].rearrange("(i p) t -> p i t", p=128)
    k.ev_store[par] = em.d(lambda d, d_=dst, xc=xcur: d.dma_start(
        d_, xc[:, :].rearrange("p (i t) -> p i t", t=T)),
        waits=[ev_ffn2_evac[NCH - 1]])


def build(taps=None):
    nc = bass.Bass("TRN2", target_bir_lowering=False)
    with contextlib.ExitStack() as es:
        k = Kern(nc, es, taps)
        _init(k)
        _weight_prep(k)
        for t in range(NT):
            _tile(k, t)
        lane_tot = dict(k.em.lane_cnt)

        def _final_wait(d, lt=lane_tot):
            last = None
            for key, tot in lt.items():
                if tot:
                    last = d.wait_ge(k.sems[key], tot)
            return last
        k.em.op("d", _final_wait, inc=0)
        block = es.enter_context(nc.Block())

        def mk(eng_key):
            def body(eng):
                maxw = {}
                for fn, waits, inc, inc_sem in k.em.ops[eng_key]:
                    for (weng, cval) in waits:
                        if weng == eng_key and eng_key in ("t", "v", "s"):
                            continue
                        if maxw.get(weng, 0) >= cval:
                            continue
                        eng.wait_ge(k.sems[weng], cval)
                        maxw[weng] = cval
                    instr = fn(eng)
                    if inc:
                        instr.then_inc(k.sems[inc_sem], inc)
            body.__name__ = "_"
            return body

        block.tensor(mk("t"))
        block.vector(mk("v"))
        block.scalar(mk("s"))
        block.sync(mk("d"))
        block.gpsimd(mk("g"))
    return nc


def _host_prep(inputs):
    x = np.asarray(inputs["L0_attn_x"], dtype=np.float32)
    def _retile(wT):
        return np.ascontiguousarray(
            wT.reshape(NCH, 128, NCH, 128).transpose(2, 1, 0, 3))

    common = {
        "wqT": _retile(np.asarray(inputs["L0_attn_wq"], np.float32).T),
        "wkT": _retile(np.asarray(inputs["L0_attn_wk"], np.float32).T),
        "wvT": _retile(np.asarray(inputs["L0_attn_wv"], np.float32).T),
        "woT": _retile(np.asarray(inputs["L0_attn_wo"], np.float32).T),
        "factors": np.stack([_retile(np.asarray(f, np.float32))
                             for f in inputs["L0_ffn_factors"]]),
        "fq1T": np.ascontiguousarray(np.asarray(inputs["L0_attn_fq1_w"], np.float32).T),
        "fq2T": np.ascontiguousarray(np.asarray(inputs["L0_attn_fq2_w"], np.float32).T),
        "fk1T": np.ascontiguousarray(np.asarray(inputs["L0_attn_fk1_w"], np.float32).T),
        "fk2T": np.ascontiguousarray(np.asarray(inputs["L0_attn_fk2_w"], np.float32).T),
        "bq1": np.ascontiguousarray(np.asarray(inputs["L0_attn_fq1_b"], np.float32).reshape(DK, 1)),
        "bq2": np.ascontiguousarray(np.asarray(inputs["L0_attn_fq2_b"], np.float32).reshape(DK, 1)),
        "bk1": np.ascontiguousarray(np.asarray(inputs["L0_attn_fk1_b"], np.float32).reshape(DK, 1)),
        "bk2": np.ascontiguousarray(np.asarray(inputs["L0_attn_fk2_b"], np.float32).reshape(DK, 1)),
        "nw": np.ascontiguousarray(np.asarray(inputs["L0_attn_norm_w"], np.float32).reshape(DK, 1)),
        "fbias": np.ascontiguousarray(np.asarray(inputs["L0_ffn_bias"], np.float32).reshape(NCH, 128).T),
    }
    in_maps = []
    for c in range(NCORES):
        m = dict(common)
        m["x"] = np.ascontiguousarray(x[c * BPC:(c + 1) * BPC].transpose(0, 2, 1))
        in_maps.append(m)
    return in_maps


def _run(inputs, taps=None):
    in_maps = _host_prep(inputs)
    nc = build(taps=taps)
    res = run_bass_kernel_spmd(nc, in_maps, core_ids=list(range(NCORES)))
    outs = [np.asarray(r["y"]).transpose(0, 2, 1) for r in res.results]
    y = np.ascontiguousarray(np.concatenate(outs, axis=0), dtype=np.float32)
    return y, res


def kernel(**inputs):
    return _run(inputs)[0]


# version 26
# speedup vs baseline: 1.0895x; 1.0895x over previous
import contextlib
import math

import numpy as np

import concourse.bass as bass
import concourse.mybir as mybir
from concourse.bass_utils import run_bass_kernel_spmd

F32 = mybir.dt.float32
BF16 = mybir.dt.bfloat16
I32 = mybir.dt.int32
AF = mybir.ActivationFunctionType
ALU = mybir.AluOpType

B, L, D = 16, 2048, 2048
NCORES = 8
BPC = B // NCORES
T = 512
TILES_PER_SEQ = L // T
NT = BPC * TILES_PER_SEQ
NCH = D // 128
H = 16
DK = 128
CPT = T // 128
DKSCALE = DK ** -0.5
FFN_C = 0.5 / math.sqrt(8.0)
EPS = 1e-5

ENGS = ("t", "v", "s", "d", "g")
PIPELINE_LN1 = False


NLANES = 12


class Em:
    def __init__(self):
        self.ops = {e: [] for e in ENGS}
        self.cnt = {e: 0 for e in ENGS}
        self.lane_cnt = {}
        for q in ("d", "g"):
            for i in range(NLANES):
                self.lane_cnt[q + str(i)] = 0
        self.lane_idx = {"d": 0, "g": 0}

    def op(self, eng, fn, waits=(), inc=0, inc_sem=None):
        ev = None
        if inc:
            if inc_sem is None:
                self.cnt[eng] += inc
                ev = (eng, self.cnt[eng])
            else:
                self.lane_cnt[inc_sem] += inc
                ev = (inc_sem, self.lane_cnt[inc_sem])
        self.ops[eng].append((fn, tuple(w for w in waits if w is not None),
                              inc, inc_sem or eng))
        return ev

    def t(self, fn, waits=(), inc=1):
        return self.op("t", fn, waits, inc)

    def v(self, fn, waits=(), inc=1):
        return self.op("v", fn, waits, inc)

    def s(self, fn, waits=(), inc=1):
        return self.op("s", fn, waits, inc)

    def _dma(self, q, fn, waits):
        l = self.lane_idx[q]
        self.lane_idx[q] = (l + 1) % NLANES
        key = q + str(l)
        w = list(waits)
        prev = self.lane_cnt[key]
        if prev:
            w.append((key, prev))
        return self.op(q, fn, w, inc=16, inc_sem=key)

    def d(self, fn, waits=(), inc=16):
        return self._dma("d", fn, waits)

    def g(self, fn, waits=(), inc=16):
        return self._dma("g", fn, waits)


class Kern:

    def __init__(self, nc, es, taps):
        self.nc = nc
        self.em = Em()
        E = es.enter_context
        self.x_in = nc.dram_tensor("x", [BPC, D, L], F32, kind="ExternalInput").ap()
        self.wT = [nc.dram_tensor(n, [NCH, 128, NCH * 128], F32, kind="ExternalInput").ap()
                   for n in ("wqT", "wkT", "wvT", "woT")]
        self.facs = nc.dram_tensor("factors", [8, NCH, 128, NCH * 128], F32,
                                   kind="ExternalInput").ap()
        self.fqkT = [nc.dram_tensor(n, [DK, DK], F32, kind="ExternalInput").ap()
                     for n in ("fq1T", "fq2T", "fk1T", "fk2T")]
        self.bin = [nc.dram_tensor(n, [DK, 1], F32, kind="ExternalInput").ap()
                    for n in ("bq1", "bq2", "bk1", "bk2")]
        self.nw_in = nc.dram_tensor("nw", [DK, 1], F32, kind="ExternalInput").ap()
        self.fb_in = nc.dram_tensor("fbias", [128, NCH], F32, kind="ExternalInput").ap()
        self.y_out = nc.dram_tensor("y", [BPC, D, L], F32, kind="ExternalOutput").ap()
        _wscr_kind = "ExternalOutput" if (taps and "wscr" in taps) else "Internal"
        self.wscr = nc.dram_tensor("wscr", [80, 128, NCH * 128], BF16,
                                   kind=_wscr_kind).ap()
        self.tap_out = {}
        self.taps = taps or {}
        for lbl, (shape, dt) in self.taps.items():
            if lbl == "wscr":
                continue
            self.tap_out[lbl] = nc.dram_tensor(
                "tap_" + lbl, shape, dt, kind="ExternalOutput").ap()

        sb = nc.sbuf_tensor
        self.ident = E(sb("ident_bf", [128, 128], BF16))
        self.iota = E(sb("iota_i", [128, 128], I32))
        self.mask = E(sb("mask_bf", [128, 128], BF16))
        self.ones_col = E(sb("ones_col", [128, 1], BF16))
        self.ones_row = E(sb("ones_row", [1, 128], BF16))
        self.zero_col = E(sb("zero_col", [128, 1], F32))
        self.eps_col = E(sb("eps_col", [128, 1], F32))
        self.fqk = [E(sb("fqk%d" % i, [128, 128], BF16)) for i in range(4)]
        self.bcols = [E(sb("bcol%d" % i, [128, 1], F32)) for i in range(4)]
        self.nw = E(sb("nw_col", [128, 1], F32))
        self.hbias = E(sb("hbias", [128, NCH], F32))
        self.xs = [E(sb("xs%d" % i, [128, NCH * T], F32)) for i in range(2)]
        self.xb = [E(sb("xb%d" % i, [128, NCH * T], BF16)) for i in range(2)]
        self.o_all = E(sb("o_all", [128, NCH * T], BF16))
        self.vh = [E(sb("vh%d" % i, [128, T], BF16)) for i in range(2)]
        self.q0h = [E(sb("q0h%d" % i, [128, T], BF16)) for i in range(2)]
        self.k0h = [E(sb("k0h%d" % i, [128, T], BF16)) for i in range(2)]
        self.vfh = [E(sb("vfh%d" % i, [128, T], BF16)) for i in range(2)]
        self.qh = [E(sb("qh%d" % i, [128, T], BF16)) for i in range(2)]
        self.kfmh = [E(sb("kfmh%d" % i, [128, T], BF16)) for i in range(2)]
        self.ktmh = [E(sb("ktmh%d" % i, [128, T], BF16)) for i in range(2)]
        self.g1f = [E(sb("g1f%d" % i, [128, T], F32)) for i in range(2)]
        self.sTm = [E(sb("sTm%d" % i, [128, 128], BF16)) for i in range(2)]
        self.state_f = E(sb("state_f", [128, H * DK], F32))
        self.state_b = [E(sb("state_b%d" % i, [128, H * DK], BF16)) for i in range(2)]
        self.NSLOT = 8
        self.wslot = [E(sb("wslot%d" % i, [128, NCH * 128], BF16))
                      for i in range(self.NSLOT)]
        self.tmp_e = E(sb("tmp_e", [128, T], F32))
        self.mbc = [E(sb("mbc%d" % i, [128, T], F32)) for i in range(2)]
        self.rbc = [E(sb("rbc%d" % i, [128, T], F32)) for i in range(2)]
        self.rrbc = E(sb("rrbc", [128, T], F32))
        self.xsq = [E(sb("xsq%d" % i, [128, T], BF16)) for i in range(2)]
        self.osq = E(sb("osq", [128, T], BF16))
        self.ones_sq = E(sb("ones_sq", [128, 128], BF16))
        self.vbc = E(sb("vbc", [128, T], F32))
        self.lnc_col = E(sb("lnc_col", [128, 1], F32))

        self.fstage = self.xs[0]
        self.facc_ap = self.xs[1][:, 0:2048]

        self.pA = E(nc.psum_tensor("pA", [128, 512], F32))
        self.pB = E(nc.psum_tensor("pB", [128, 512], F32))
        self.pTT = E(nc.psum_tensor("pTT", [128, 1024], BF16))
        self.pS = [E(nc.psum_tensor("pS0", [128, 512], F32)),
                   E(nc.psum_tensor("pS1", [128, 512], F32))]
        self.pO = E(nc.psum_tensor("pO", [128, 512], F32))
        self.pK = [E(nc.psum_tensor("pK0", [128, 512], F32)),
                   E(nc.psum_tensor("pK1", [128, 512], F32))]
        self.pAB = [self.pA, self.pB]

        self.sems = {e: E(nc.semaphore("sem_" + e)) for e in ENGS}
        for q in ("d", "g"):
            for i in range(NLANES):
                key = q + str(i)
                self.sems[key] = E(nc.semaphore("sem_" + key))

        self.guard = {}
        self.slot_free = [None] * self.NSLOT
        self.slot_store = [None] * self.NSLOT
        self.scr_built = {}
        self.fsl_free = [None, None, None]
        self.ev_fcast = None
        self.slot_rr = 0
        self.ev_store = [None, None]
        self.ev_ffn2_mml = None
        self.ev_o_consumed = None
        self.ev_attn_done = None
        self.ev_last_cast = None
        self.cast_ev = [[None] * H, [None] * H]
        self.inter_ev = [[None] * H, [None] * H]
        self.xsq_guard = [None, None]
        self.sT_guard = [None, None]
        self.sTm_free = [None, None]
        self.vh_free = [None, None]
        self.kv_guard = [None, None]
        self.hfree = {n: [None, None] for n in
                      ("q0h", "k0h", "vfh", "qh", "kfm", "ktm", "g1f")}
        self.sw = [None, None]
        self.last_v0 = None
        self.osq_free = None
        self.rrbc_free = None
        self.rbc_free = [None, None]
        self.mbc_free = [None, None]
        self.cur_xb = None
        self.cur_rbc = None
        self.cur_par = 0
        self.next_ln1 = None
        self.pre_xload = None
        self.ev_xt = [None] * NCH
        self.omult = [None] * H

    def ch(self, buf, i, w=T):
        return buf[:, i * w:(i + 1) * w]

    def c128(self, buf, c):
        return buf[:, c * 128:(c + 1) * 128]

    def _fsl_ap(self, s):
        return self.xs[1][:, (s + 1) * 2048:(s + 2) * 2048]

    def _build_fcol(self, j, s_):
        em = self.em
        k = self

        def ld(r):
            sl2 = r % 3
            src_ = k.facs[r, j]
            q = em.g if r % 2 == 0 else em.d
            return q(lambda g, sr=src_, s2=sl2: g.dma_start(
                k._fsl_ap(s2), sr), waits=[k.fsl_free[sl2]])

        ev_ld = [ld(0), ld(1), ld(2)]
        ev_add = em.v(lambda v: v.tensor_tensor(
            k.facc_ap, k._fsl_ap(0), k._fsl_ap(1), ALU.add),
            waits=[ev_ld[0], ev_ld[1], k.ev_fcast])
        k.fsl_free[0] = ev_add
        k.fsl_free[1] = ev_add
        for r in range(2, 8):
            if r + 1 < 8:
                ev_ld.append(ld(r + 1))
            sl2 = r % 3
            ev_add = em.v(lambda v, s2=sl2: v.tensor_tensor(
                k.facc_ap, k.facc_ap, k._fsl_ap(s2), ALU.add),
                waits=[ev_ld[r], ev_add])
            k.fsl_free[sl2] = ev_add
        ev_c = em.g(lambda g, sl=s_: g.dma_start(
            k.wslot[sl][:, :], k.facc_ap),
            waits=[ev_add, k.slot_free[s_], k.slot_store[s_]])
        k.ev_fcast = ev_c
        return ev_c

    def load_col(self, mat, j):
        s_ = self.slot_rr
        self.slot_rr = (s_ + 1) % self.NSLOT
        row = mat * 16 + j
        if (mat, j) not in self.scr_built:
            if mat == 4:
                ev = self._build_fcol(j, s_)
            else:
                src_ = self.wT[mat][j]
                ev = self.em.g(lambda g, sr=src_, sl=s_: g.dma_start(
                    self.wslot[sl][:, :], sr),
                    waits=[self.slot_free[s_], self.slot_store[s_]])
            ev_st = self.em.d(lambda d, sl=s_, r_=row: d.dma_start(
                self.wscr[r_], self.wslot[sl][:, :]), waits=[ev])
            self.slot_store[s_] = ev_st
            self.scr_built[(mat, j)] = ev_st
            return s_, ev
        q = self.em.d
        ev = q(lambda d, r_=row, sl=s_: d.dma_start(
            self.wslot[sl][:, :], self.wscr[r_]),
            waits=[self.slot_free[s_], self.slot_store[s_],
                   self.scr_built[(mat, j)]])
        self.slot_store[s_] = None
        return s_, ev

    def tap(self, lbl, buf_ap, waits):
        if lbl in self.tap_out:
            return self.em.d(lambda d, b_=buf_ap, o_=self.tap_out[lbl]:
                             d.dma_start(o_, b_), waits=waits)
        return None


def _init(k):
    em, nc = k.em, k.nc
    ev = em.g(lambda g: g.iota(k.iota[:, :], [[1, 128]], base=0,
                               channel_multiplier=-1), inc=1)
    em.g(lambda g: g.memset(k.ones_col[:, :], 1.0), inc=1)
    em.g(lambda g: g.memset(k.ones_row[:, :], 1.0), inc=1)
    em.g(lambda g: g.memset(k.zero_col[:, :], 0.0), inc=1)
    em.g(lambda g: g.memset(k.eps_col[:, :], EPS), inc=1)
    em.g(lambda g: g.memset(k.ones_sq[:, :], 1.0), inc=1)
    em.g(lambda g: g.memset(k.lnc_col[:, :], math.log(FFN_C)), inc=1)
    em.g(lambda g: g.memset(k.state_f[:, :], 0.0), inc=1)
    em.g(lambda g: g.memset(k.state_b[0][:, :], 0.0), inc=1)
    ev_gi = em.g(lambda g: g.memset(k.state_b[1][:, :], 0.0), inc=1)
    k.ev_mask = em.v(lambda v: v.tensor_scalar(
        k.mask[:, :], k.iota[:, :], 0, None, ALU.is_ge), waits=[ev_gi])
    k.ev_ident = em.v(lambda v: v.tensor_scalar(
        k.ident[:, :], k.iota[:, :], 0, None, ALU.is_equal), waits=[ev_gi])
    k.ev_ginit = ev_gi
    k.cast_ev = [[ev_gi] * H, [ev_gi] * H]
    k.ev_last_cast = ev_gi

    evl = []
    for i in range(4):
        evl.append(em.d(lambda d, i_=i: d.dma_start(
            k.vbc[:, i_ * 128:(i_ + 1) * 128], k.fqkT[i_])))
    evb = []
    for i in range(4):
        evb.append(em.d(lambda d, i_=i: d.dma_start(k.bcols[i_][:, :], k.bin[i_])))
    ev_nw = em.d(lambda d: d.dma_start(k.nw[:, :], k.nw_in))
    ev_fb = em.d(lambda d: d.dma_start(k.hbias[:, :], k.fb_in))
    em.s(lambda s: s.mul(k.fqk[0][:, :], k.vbc[:, 0:128], DKSCALE), waits=[evl[0]])
    for i in (1, 2, 3):
        em.s(lambda s, i_=i: s.copy(
            k.fqk[i_][:, :], k.vbc[:, i_ * 128:(i_ + 1) * 128]), waits=[evl[i]])
    ev_fqk = em.s(lambda s: s.mul(k.bcols[0][:, :], k.bcols[0][:, :], DKSCALE),
                  waits=[evb[0]])
    k.ev_hb = em.s(lambda s: s.mul(k.hbias[:, :], k.hbias[:, :], 0.5),
                   waits=[ev_fb, evb[1], evb[2], evb[3], ev_nw])
    k.ev_smallw = ev_fqk
    src0 = k.x_in[0, :, 0:T].rearrange("(i p) t -> p i t", p=128)
    k.ev_xload0 = em.d(lambda d, s_=src0: d.dma_start(
        k.xs[0][:, :].rearrange("p (i t) -> p i t", t=T), s_))


def _weight_prep(k):
    pass


def _ln_pass(k, src_buf, ffn_scaled, ev_src_ready, xb_guard, par):
    em = k.em
    g = k.guard
    xb = k.xb[par]
    mbc = k.mbc[par]
    rbc = k.rbc[par]
    ev_cast = [None] * NCH
    ev_statmm = [None] * NCH
    for i in range(NCH):
        if isinstance(ev_src_ready, list):
            w_src = ev_src_ready[i]
        else:
            w_src = ev_src_ready
        ev_cast[i] = em.s(lambda s, i_=i, sb=src_buf: s.copy(
            k.ch(xb, i_), k.ch(sb, i_)), waits=[w_src, xb_guard])
        ev_sq = em.s(lambda s, i_=i: s.activation(
            k.xsq[i_ % 2][:, :], k.ch(xb, i_), AF.Square,
            bias=k.zero_col[:, :], scale=1.0), waits=[k.xsq_guard[i % 2]])
        em.t(lambda tt, i_=i: tt.matmul(
            k.pK[0][:, 0:T], k.ones_sq[:, :], k.ch(xb, i_),
            start=(i_ == 0), stop=(i_ == NCH - 1)),
            waits=[ev_cast[i], g.get("pK0"), k.ev_ginit], inc=0)
        ev_statmm[i] = em.t(lambda tt, i_=i: tt.matmul(
            k.pK[1][:, 0:T], k.ones_sq[:, :], k.xsq[i_ % 2][:, :],
            start=(i_ == 0), stop=(i_ == NCH - 1)),
            waits=[ev_sq, g.get("pK1"), k.ev_ginit])
        k.xsq_guard[i % 2] = ev_statmm[i]
    ev_stats = ev_statmm[NCH - 1]
    ev_mbc = em.s(lambda s: s.mul(mbc[:, :], k.pK[0][:, 0:T], 1.0 / D),
                  waits=[ev_stats, k.mbc_free[par]])
    g["pK0"] = ev_mbc
    ev_msq = em.s(lambda s: s.activation(
        k.vbc[:, :], mbc[:, :], AF.Square, bias=k.zero_col[:, :], scale=1.0))
    ev_vbc = em.v(lambda v: v.scalar_tensor_tensor(
        k.vbc[:, :], k.pK[1][:, 0:T], 1.0 / D, k.vbc[:, :],
        ALU.mult, ALU.subtract), waits=[ev_msq, ev_stats])
    g["pK1"] = ev_vbc
    ev_lnv = em.s(lambda s: s.activation(
        k.vbc[:, :], k.vbc[:, :], AF.Ln, bias=k.eps_col[:, :], scale=1.0),
        waits=[ev_vbc])
    bias_ap = k.lnc_col if ffn_scaled else k.zero_col
    ev_rbc = em.s(lambda s, ba=bias_ap: s.activation(
        rbc[:, :], k.vbc[:, :], AF.Exp, bias=ba[:, :], scale=-0.5),
        waits=[ev_lnv, k.rbc_free[par]])
    ev_xt = [None] * NCH
    for i in range(NCH):
        ev_xt[i] = em.v(lambda v, i_=i, sb=src_buf: v.tensor_tensor(
            k.ch(xb, i_), k.ch(sb, i_), mbc[:, :], ALU.subtract),
            waits=[ev_mbc, ev_statmm[i]])
    k.mbc_free[par] = ev_xt[NCH - 1]
    return ev_xt, ev_rbc


def _ffn_sweep(k, mat, xcur, xb, ev_xt, rbc, par):
    em = k.em
    ev_evac = [None] * NCH
    ev_mml = None
    for j in range(NCH):
        sl, ev_w = k.load_col(mat, j)
        pj = k.pAB[j % 2]
        bank = "pA" if j % 2 == 0 else "pB"
        for i in range(NCH):
            w = [ev_xt[i]]
            if i == 0:
                w += [ev_w, k.guard.get(bank)]
            ev_mml = em.t(lambda tt, i_=i, sl_=sl, p_=pj: tt.matmul(
                p_[:, :], k.wslot[sl_][:, i_ * 128:(i_ + 1) * 128],
                k.ch(xb, i_), start=(i_ == 0), stop=(i_ == NCH - 1)),
                waits=w, inc=1 if i == NCH - 1 else 0)
        k.slot_free[sl] = ev_mml
        em.v(lambda v, p_=pj: v.tensor_tensor(
            k.tmp_e[:, :], p_[:, :], rbc[:, :], ALU.mult),
            waits=[ev_mml, k.ev_hb], inc=0)
        ev_evac[j] = em.v(lambda v, j_=j, xc=xcur: v.scalar_tensor_tensor(
            k.ch(xc, j_), k.tmp_e[:, :], k.hbias[:, j_:j_ + 1],
            k.ch(xc, j_), ALU.add, ALU.add))
        k.guard[bank] = ev_evac[j]
    k.rbc_free[par] = ev_evac[NCH - 1]
    return ev_evac, ev_mml


def _head_sweeps(k, h):
    em = k.em
    g = k.guard
    par = h % 2
    xbc = k.cur_xb
    rbcc = k.cur_rbc

    def sweep(mat, pbank, bankkey, extra0):
        sl, ev_w = k.load_col(mat, h)
        ev = None
        for half in range(2):
            for i in range(half * 8, half * 8 + 8):
                w = [k.ev_xt[i]]
                if i == 0:
                    w += [ev_w, g.get(bankkey)] + extra0
                ev = em.t(lambda tt, i_=i, sl_=sl, p_=pbank: tt.matmul(
                    p_[:, :], k.wslot[sl_][:, i_ * 128:(i_ + 1) * 128],
                    k.ch(xbc, i_), start=(i_ == 0), stop=(i_ == NCH - 1)),
                    waits=w, inc=1 if i == NCH - 1 else 0)
            if half == 1:
                k.slot_free[sl] = ev
            yield ev

    it = sweep(0, k.pA, "pA", [])
    next(it)
    yield
    ev_q0 = next(it)
    ev_q0e = em.v(lambda v: v.tensor_tensor(
        k.q0h[par][:, :], k.pA[:, :], rbcc[:, :], ALU.mult),
        waits=[ev_q0, k.hfree["q0h"][par]])
    g["pA"] = ev_q0e
    yield
    it = sweep(1, k.pB, "pB", [])
    next(it)
    yield
    ev_k0 = next(it)
    ev_k0e = em.v(lambda v: v.tensor_tensor(
        k.k0h[par][:, :], k.pB[:, :], rbcc[:, :], ALU.mult),
        waits=[ev_k0, k.hfree["k0h"][par]])
    g["pB"] = ev_k0e
    yield
    it = sweep(2, k.pA, "pA", [])
    next(it)
    yield
    ev_v0 = next(it)
    ev_vfme = em.v(lambda v: v.tensor_tensor(
        k.vfh[par][:, :], k.pA[:, :], rbcc[:, :], ALU.mult),
        waits=[ev_v0, k.hfree["vfh"][par]])
    g["pA"] = ev_vfme
    k.rbc_free[k.cur_par] = ev_vfme
    k.sw[par] = (ev_q0e, ev_k0e, ev_vfme)
    k.last_v0 = ev_v0


def _head_attn(k, h):
    em = k.em
    g = k.guard
    par = h % 2
    ev_q0e, ev_k0e, ev_vfme = k.sw[par]
    q0h, k0h, vfh = k.q0h[par], k.k0h[par], k.vfh[par]
    qh, kfmh, ktmh, g1f, vhb = (k.qh[par], k.kfmh[par], k.ktmh[par],
                                k.g1f[par], k.vh[par])
    ev_fq1 = em.t(lambda tt: tt.matmul(
        k.pS[0][:, :], k.fqk[0][:, :], q0h[:, :], start=True, stop=True),
        waits=[ev_q0e, k.sT_guard[0], k.ev_smallw])
    ev_fq2 = em.t(lambda tt: tt.matmul(
        k.pS[1][:, :], k.fqk[1][:, :], q0h[:, :], start=True, stop=True),
        waits=[k.sT_guard[1]])
    ev_g1 = em.s(lambda s: s.activation(
        g1f[:, :], k.pS[0][:, :], AF.Identity, bias=k.bcols[0][:, :],
        scale=1.0), waits=[ev_fq1, k.hfree["g1f"][par]])
    ev_qh = em.v(lambda v: v.scalar_tensor_tensor(
        qh[:, :], k.pS[1][:, :], k.bcols[1][:, :], g1f[:, :],
        ALU.add, ALU.mult),
        waits=[ev_fq2, ev_g1, k.hfree["qh"][par]])
    yield
    ev_fk1 = em.t(lambda tt: tt.matmul(
        k.pS[0][:, :], k.fqk[2][:, :], k0h[:, :], start=True, stop=True),
        waits=[ev_g1, ev_k0e])
    ev_fk2 = em.t(lambda tt: tt.matmul(
        k.pS[1][:, :], k.fqk[3][:, :], k0h[:, :], start=True, stop=True),
        waits=[ev_qh])
    ev_g1k = em.s(lambda s: s.activation(
        g1f[:, :], k.pS[0][:, :], AF.Identity, bias=k.bcols[2][:, :],
        scale=1.0), waits=[ev_fk1, ev_qh])
    ev_kfm = em.v(lambda v: v.scalar_tensor_tensor(
        kfmh[:, :], k.pS[1][:, :], k.bcols[3][:, :], g1f[:, :],
        ALU.add, ALU.mult),
        waits=[ev_fk2, ev_g1k, k.hfree["kfm"][par]])
    k.hfree["g1f"][par] = ev_kfm
    k.hfree["q0h"][par] = ev_fq2
    k.hfree["k0h"][par] = ev_fk2
    yield
    for c in range(CPT):
        em.t(lambda tt, c_=c: tt.transpose(
            k.pTT[:, c_ * 128:(c_ + 1) * 128], k.c128(kfmh, c_), k.ident[:, :]),
            waits=[ev_kfm, k.ev_ident, g.get("pTT")] if c == 0 else [], inc=0)
    ev_tr = None
    for c in range(CPT):
        ev_tr = em.t(lambda tt, c_=c: tt.transpose(
            k.pTT[:, 512 + c_ * 128:512 + (c_ + 1) * 128],
            k.c128(vfh, c_), k.ident[:, :]),
            waits=[ev_vfme] if c == 0 else [], inc=1 if c == CPT - 1 else 0)
    k.hfree["vfh"][par] = ev_tr
    ev_ktm = em.v(lambda v: v.tensor_copy(ktmh[:, :], k.pTT[:, 0:512]),
                  waits=[ev_tr, k.hfree["ktm"][par]])
    ev_vtm = em.v(lambda v: v.tensor_copy(
        vhb[:, :], k.pTT[:, 512:1024]), waits=[k.hfree["vh"][par] if False
                                               else k.vh_free[par]])
    g["pTT"] = ev_vtm
    ev_sT = [None] * CPT
    ev_maskd = [None] * CPT
    ev_inter = [None] * CPT
    ev_intra = [None] * CPT
    ev_addv = [None] * CPT
    ev_cast = [None] * CPT

    def emit_sT(c):
        gp = c % 2
        w_extra = [ev_g1k] if c == 0 else ([ev_kfm] if c == 1 else [])
        ev_sT[c] = em.t(lambda tt, c_=c, g_=gp: tt.matmul(
            k.pS[g_][:, 0:128], k.c128(kfmh, c_), k.c128(qh, c_),
            start=True, stop=True),
            waits=[ev_qh, ev_kfm, k.sT_guard[gp]] + w_extra)
        ev_maskd[c] = em.v(lambda v, c_=c, g_=gp: v.tensor_tensor(
            k.sTm[g_][:, :], k.pS[g_][:, 0:128], k.mask[:, :], ALU.mult),
            waits=[ev_sT[c], k.ev_mask, k.sTm_free[gp]])
        k.sT_guard[gp] = ev_maskd[c]

    emit_sT(0)
    for c in range(CPT):
        yield
        gp = c % 2
        if c + 1 < CPT:
            emit_sT(c + 1)
        ev_kv = em.t(lambda tt, c_=c, g_=gp: tt.matmul(
            k.pK[g_][:, 0:128], k.c128(ktmh, c_), k.c128(vhb, c_),
            start=True, stop=True),
            waits=[ev_ktm, ev_vtm, k.kv_guard[gp], g.get("pK" + str(gp))])
        g["pK" + str(gp)] = None
        po_guard = None
        if c == 0:
            po_guard = k.omult[h - 1] if h > 0 else (g.get("pO") or k.ev_o_consumed)
        ev_intra[c] = em.t(lambda tt, c_=c, g_=gp: tt.matmul(
            k.c128(k.pO, c_), k.c128(vhb, c_), k.sTm[g_][:, :],
            start=True, stop=False), waits=[ev_maskd[c], po_guard])
        k.sTm_free[gp] = ev_intra[c]
        ev_inter[c] = em.t(lambda tt, c_=c, h_=h, sb=k.state_b[c % 2]: tt.matmul(
            k.c128(k.pO, c_), sb[:, h_ * 128:(h_ + 1) * 128], k.c128(qh, c_),
            start=False, stop=True),
            waits=[k.cast_ev[c % 2][h]])
        k.inter_ev[c % 2][h] = ev_inter[c]
        ev_addv[c] = em.v(lambda v, h_=h, g_=gp: v.tensor_tensor(
            k.state_f[:, h_ * 128:(h_ + 1) * 128],
            k.state_f[:, h_ * 128:(h_ + 1) * 128],
            k.pK[g_][:, 0:128], ALU.add),
            waits=[ev_kv, ev_cast[c - 1] if c > 0 else k.ev_last_cast])
        k.kv_guard[gp] = ev_addv[c]
        ev_cast[c] = em.s(lambda s, c_=c, h_=h: s.copy(
            k.state_b[(c_ + 1) % 2][:, h_ * 128:(h_ + 1) * 128],
            k.state_f[:, h_ * 128:(h_ + 1) * 128]),
            waits=[ev_addv[c], k.inter_ev[(c + 1) % 2][h]])
        k.cast_ev[(c + 1) % 2][h] = ev_cast[c]
    k.hfree["kfm"][par] = ev_sT[CPT - 1]
    k.ev_last_cast = ev_cast[CPT - 1]
    k.hfree["qh"][par] = ev_inter[CPT - 1]
    k.hfree["ktm"][par] = ev_inter[CPT - 1]
    k.vh_free[par] = ev_inter[CPT - 1]
    k.ev_attn_done = ev_inter[CPT - 1]
    yield
    ev_osq = em.s(lambda s: s.activation(
        k.osq[:, :], k.pO[:, :], AF.Square, bias=k.zero_col[:, :], scale=1.0),
        waits=[ev_inter[CPT - 1], k.osq_free])
    ev_ssq = em.t(lambda tt: tt.matmul(
        k.pK[1][:, 0:T], k.ones_sq[:, :], k.osq[:, :], start=True, stop=True),
        waits=[ev_osq, k.kv_guard[1], g.get("pK1")])
    g["pK1"] = None
    k.osq_free = ev_ssq
    ev_lnr = em.s(lambda s: s.activation(
        k.rrbc[:, :], k.pK[1][:, 0:T], AF.Ln, bias=k.eps_col[:, :],
        scale=1.0 / DK), waits=[ev_ssq, k.rrbc_free])
    k.kv_guard[1] = ev_lnr
    ev_rrbc = em.s(lambda s: s.activation(
        k.rrbc[:, :], k.rrbc[:, :], AF.Exp, bias=k.zero_col[:, :],
        scale=-0.5), waits=[ev_lnr])
    k.omult[h] = em.v(lambda v, h_=h: v.scalar_tensor_tensor(
        k.ch(k.o_all, h_), k.pO[:, :], k.nw[:, :], k.rrbc[:, :],
        ALU.mult, ALU.mult),
        waits=[ev_rrbc, k.ev_o_consumed, k.ev_hb])
    k.rrbc_free = k.omult[h]
    g["pO"] = k.omult[h]


def _tile(k, t):
    em = k.em
    b = t // TILES_PER_SEQ
    t0 = (t % TILES_PER_SEQ) * T
    par = t % 2
    xcur = k.xs[par]
    if t == 0:
        ev_xload = k.ev_xload0
    elif PIPELINE_LN1:
        ev_xload = k.pre_xload
    else:
        srcx = k.x_in[b, :, t0:t0 + T].rearrange("(i p) t -> p i t", p=128)
        ev_xload = em.d(lambda d, s_=srcx, xc=xcur: d.dma_start(
            xc[:, :].rearrange("p (i t) -> p i t", t=T), s_),
            waits=[k.ev_store[par]])
    if t > 0 and t % TILES_PER_SEQ == 0:
        em.v(lambda v: v.memset(k.state_f[:, :], 0.0),
             waits=[k.ev_last_cast, k.ev_attn_done], inc=0)
        ev_z = em.v(lambda v: v.memset(k.state_b[0][:, :], 0.0))
        k.cast_ev = [[ev_z] * H, [ev_z] * H]
        k.ev_last_cast = ev_z
    if t == 0 or not PIPELINE_LN1:
        ev_xt1, ev_rbc1 = _ln_pass(k, xcur, True, ev_xload,
                                   k.ev_ffn2_mml, par)
    else:
        ev_xt1, ev_rbc1 = k.next_ln1
    ev_ffn1_evac, ev_ffn1_mml = _ffn_sweep(k, 4, xcur, k.xb[par], ev_xt1,
                                           k.rbc[par], par)
    if t == 0:
        k.ev_store[1] = k.ev_fcast
    if t == 0:
        k.tap("x2", xcur[:, :], [ev_ffn1_evac[NCH - 1]])
        k.tap("rbc1", k.rbc[par][:, :], [ev_rbc1])
        k.tap("xt1", k.xb[par][:, :], [ev_xt1[NCH - 1]])
    ev_xt2, ev_rbc2 = _ln_pass(k, xcur, False, ev_ffn1_evac, ev_ffn1_mml, par)
    k.cur_xb = k.xb[par]
    k.cur_rbc = k.rbc[par]
    k.cur_par = par
    k.ev_xt = ev_xt2
    def _drain(gen):
        if gen is not None:
            for _ in gen:
                pass

    def _drive(a, b):
        while True:
            if a is not None:
                try:
                    next(a)
                except StopIteration:
                    a = None
            if b is not None:
                try:
                    next(b)
                except StopIteration:
                    b = None
            if a is None and b is None:
                return

    _drain(_head_sweeps(k, 0))
    for h in range(H):
        ga = _head_sweeps(k, h + 1) if h + 1 < H else None
        gb = _head_attn(k, h)
        _drive(ga, gb)
    ev_v0_last = k.last_v0
    k.guard["pK0"] = k.kv_guard[0]
    k.guard["pK1"] = k.kv_guard[1]
    if PIPELINE_LN1 and t + 1 < NT:
        npar = (t + 1) % 2
        nb = (t + 1) // TILES_PER_SEQ
        nt0 = ((t + 1) % TILES_PER_SEQ) * T
        nsrc = k.x_in[nb, :, nt0:nt0 + T].rearrange("(i p) t -> p i t", p=128)
        ev_nx = em.d(lambda d, s_=nsrc, xc=k.xs[npar]: d.dma_start(
            xc[:, :].rearrange("p (i t) -> p i t", t=T), s_),
            waits=[k.ev_store[npar]])
        k.pre_xload = ev_nx
        k.next_ln1 = _ln_pass(k, k.xs[npar], True, ev_nx,
                              k.ev_ffn2_mml, npar)
    ev_wo_evac = [None] * NCH
    ev_wo_mml = None
    for j in range(NCH):
        sl, ev_w = k.load_col(3, j)
        pj = k.pAB[j % 2]
        bank = "pA" if j % 2 == 0 else "pB"
        for i in range(NCH):
            w = [k.omult[i]]
            if i == 0:
                w += [ev_w, k.guard.get(bank)]
            ev_wo_mml = em.t(lambda tt, i_=i, sl_=sl, p_=pj: tt.matmul(
                p_[:, :], k.wslot[sl_][:, i_ * 128:(i_ + 1) * 128],
                k.ch(k.o_all, i_), start=(i_ == 0), stop=(i_ == NCH - 1)),
                waits=w, inc=1 if i == NCH - 1 else 0)
        k.slot_free[sl] = ev_wo_mml
        ev_wo_evac[j] = em.v(lambda v, j_=j, p_=pj, xc=xcur: v.tensor_tensor(
            k.ch(xc, j_), p_[:, :], k.ch(xc, j_), ALU.add),
            waits=[ev_wo_mml])
        k.guard[bank] = ev_wo_evac[j]
    k.ev_o_consumed = ev_wo_mml
    k.tap("x3", xcur[:, :], [ev_wo_evac[NCH - 1]])
    if t == 0:
        k.tap("x3", xcur[:, :], [ev_wo_evac[NCH - 1]])
        k.tap("oall", k.o_all[:, :], [k.omult[H - 1]])
    ev_xt3, ev_rbc3 = _ln_pass(k, xcur, True, ev_wo_evac, ev_v0_last, par)
    ev_ffn2_evac, k.ev_ffn2_mml = _ffn_sweep(k, 4, xcur, k.xb[par], ev_xt3,
                                             k.rbc[par], par)
    dst = k.y_out[b, :, t0:t0 + T].rearrange("(i p) t -> p i t", p=128)
    k.ev_store[par] = em.d(lambda d, d_=dst, xc=xcur: d.dma_start(
        d_, xc[:, :].rearrange("p (i t) -> p i t", t=T)),
        waits=[ev_ffn2_evac[NCH - 1]])


def build(taps=None):
    nc = bass.Bass("TRN2", target_bir_lowering=False)
    with contextlib.ExitStack() as es:
        k = Kern(nc, es, taps)
        _init(k)
        _weight_prep(k)
        for t in range(NT):
            _tile(k, t)
        lane_tot = dict(k.em.lane_cnt)

        def _final_wait(d, lt=lane_tot):
            last = None
            for key, tot in lt.items():
                if tot:
                    last = d.wait_ge(k.sems[key], tot)
            return last
        k.em.op("d", _final_wait, inc=0)
        block = es.enter_context(nc.Block())

        def mk(eng_key):
            def body(eng):
                maxw = {}
                for fn, waits, inc, inc_sem in k.em.ops[eng_key]:
                    for (weng, cval) in waits:
                        if weng == eng_key and eng_key in ("t", "v", "s"):
                            continue
                        if maxw.get(weng, 0) >= cval:
                            continue
                        eng.wait_ge(k.sems[weng], cval)
                        maxw[weng] = cval
                    instr = fn(eng)
                    if inc:
                        instr.then_inc(k.sems[inc_sem], inc)
            body.__name__ = "_"
            return body

        block.tensor(mk("t"))
        block.vector(mk("v"))
        block.scalar(mk("s"))
        block.sync(mk("d"))
        block.gpsimd(mk("g"))
    return nc


def _host_prep(inputs):
    x = np.asarray(inputs["L0_attn_x"], dtype=np.float32)
    def _retile(wT):
        return np.ascontiguousarray(
            wT.reshape(NCH, 128, NCH, 128).transpose(2, 1, 0, 3))

    common = {
        "wqT": _retile(np.asarray(inputs["L0_attn_wq"], np.float32).T),
        "wkT": _retile(np.asarray(inputs["L0_attn_wk"], np.float32).T),
        "wvT": _retile(np.asarray(inputs["L0_attn_wv"], np.float32).T),
        "woT": _retile(np.asarray(inputs["L0_attn_wo"], np.float32).T),
        "factors": np.stack([_retile(np.asarray(f, np.float32))
                             for f in inputs["L0_ffn_factors"]]),
        "fq1T": np.ascontiguousarray(np.asarray(inputs["L0_attn_fq1_w"], np.float32).T),
        "fq2T": np.ascontiguousarray(np.asarray(inputs["L0_attn_fq2_w"], np.float32).T),
        "fk1T": np.ascontiguousarray(np.asarray(inputs["L0_attn_fk1_w"], np.float32).T),
        "fk2T": np.ascontiguousarray(np.asarray(inputs["L0_attn_fk2_w"], np.float32).T),
        "bq1": np.ascontiguousarray(np.asarray(inputs["L0_attn_fq1_b"], np.float32).reshape(DK, 1)),
        "bq2": np.ascontiguousarray(np.asarray(inputs["L0_attn_fq2_b"], np.float32).reshape(DK, 1)),
        "bk1": np.ascontiguousarray(np.asarray(inputs["L0_attn_fk1_b"], np.float32).reshape(DK, 1)),
        "bk2": np.ascontiguousarray(np.asarray(inputs["L0_attn_fk2_b"], np.float32).reshape(DK, 1)),
        "nw": np.ascontiguousarray(np.asarray(inputs["L0_attn_norm_w"], np.float32).reshape(DK, 1)),
        "fbias": np.ascontiguousarray(np.asarray(inputs["L0_ffn_bias"], np.float32).reshape(NCH, 128).T),
    }
    in_maps = []
    for c in range(NCORES):
        m = dict(common)
        m["x"] = np.ascontiguousarray(x[c * BPC:(c + 1) * BPC].transpose(0, 2, 1))
        in_maps.append(m)
    return in_maps


def _run(inputs, taps=None):
    in_maps = _host_prep(inputs)
    nc = build(taps=taps)
    res = run_bass_kernel_spmd(nc, in_maps, core_ids=list(range(NCORES)))
    outs = [np.asarray(r["y"]).transpose(0, 2, 1) for r in res.results]
    y = np.ascontiguousarray(np.concatenate(outs, axis=0), dtype=np.float32)
    return y, res


def kernel(**inputs):
    return _run(inputs)[0]
